# version 24
# speedup vs baseline: 1.2152x; 1.0013x over previous
import numpy as np
import ml_dtypes
import concourse.bass as bass
import concourse.mybir as mybir
from concourse.bass_utils import run_bass_kernel_spmd

F32 = mybir.dt.float32
BF16 = mybir.dt.bfloat16
AF = mybir.ActivationFunctionType
ALU = mybir.AluOpType
AX = mybir.AxisListType.X

D, H, DH, CK, FF, RANK, EPS = 1024, 16, 64, 64, 4096, 256, 1e-5
NCORE = 8


def ecopy(e, out, in_):
    if hasattr(e, "tensor_copy"):
        return e.tensor_copy(out=out, in_=in_)
    return e.copy(out=out, in_=in_)


def bc(ap, n):
    return bass.AP(tensor=ap.tensor, offset=ap.offset,
                   ap=list(ap.ap) + [[0, n]])


class Sched:
    ENGS = ("t", "v", "s", "g", "q")

    def __init__(self):
        self.items = []
        self.cnt = {e: 0 for e in self.ENGS}
        self.dcnt = {"dq": 0, "dg": 0, "dg2": 0}

    def op(self, eng, fn, deps=(), selfdep=True):
        waits = [d for d in deps if d is not None]
        if eng in ("v", "s", "g") and selfdep and self.cnt[eng] > 0:
            waits.append((eng, self.cnt[eng]))
        self.cnt[eng] += 1
        self.items.append((eng, fn, waits, eng, 1))
        return (eng, self.cnt[eng])

    def mm(self, fn, deps=()):
        return self.op("t", fn, deps, selfdep=False)

    def dma(self, q, fn, deps=(), sem=None):
        eng = "q" if q == "q" else "g"
        if sem is None:
            sem = "dq" if q == "q" else "dg"
        self.dcnt[sem] += 16
        self.items.append((eng, fn, [d for d in deps if d is not None], sem, 16))
        return (sem, self.dcnt[sem])

    def barrier(self):
        t = [(e, self.cnt[e]) for e in self.ENGS if self.cnt[e] > 0]
        t += [(s, self.dcnt[s])
              for s in ("dq", "dg", "dg2") if self.dcnt[s] > 0]
        return t

    def emit_engine(self, eng, raw, sems):
        dw = {}
        for it in self.items:
            if it[0] != eng:
                continue
            (_, fn, waits, isem, iamt) = it[:5]
            for (sem, val) in waits:
                if dw.get(sem, 0) >= val:
                    continue
                raw.wait_ge(sems[sem], val)
                dw[sem] = val
            fn(raw).then_inc(sems[isem], iamt)


def build(nc, n_tiles=8, tile_s=512, seq=2048, nb=2, dumps=None, skip_scratch=False):
    dumps = dumps or {}
    GR = tile_s // 128
    NCH = tile_s // CK
    TPB = seq // tile_s

    dram = {}

    def din(name, shape, dtype=F32):
        dram[name] = nc.dram_tensor(name, list(shape), dtype,
                                    kind="ExternalInput").ap()
        return dram[name]

    din("xfm", (nb, TPB, 8, 128, tile_s))
    out_hbm = nc.dram_tensor("out", [nb, seq, D], F32, kind="ExternalOutput").ap()
    for nm, sh in [("wq", (D, D)), ("wk", (D, D)), ("wv", (D, D)),
                   ("wo", (D, D)), ("wbg", (D, 2 * H)), ("w1", (D, FF)),
                   ("w2", (FF, D)), ("u1", (D, RANK)), ("v1", (RANK, FF)),
                   ("u2", (FF, RANK)), ("v2", (RANK, D)),
                   ("g1c", (128, 8)), ("g2c", (128, 8)),
                   ("b1ac", (128, 32)), ("b1fc", (128, 32)),
                   ("b2ac", (128, 8)), ("bocc", (128, 8)), ("b2fr", (1, D)),
                   ("c_ones1rf", (1, 128)), ("c_id128f", (128, 128)),
                   ("c_fvals", (128, 16)), ("c_cums", (128, 256))]:
        din(nm, sh)
    for nm, sh in [("c_auginit", (128, H * 64)),
                   ("c_ones128", (128, 1)), ("c_ones1r", (1, 128)),
                   ("c_esel", (48, 3 * 8 * 128)), ("b1lcb", (128, 8)),
                   ("b2lcb", (128, 8)), ("cm_tril_s", (64, 512)),
                   ("cm_triu_s", (64, 512)), ("cm_triu_i", (64, 512)),
                   ("cm_irep", (128, 512))]:
        din(nm, sh, BF16)

    scr = {"s_wo": dram["wo"], "s_w1": dram["w1"], "s_w2": dram["w2"],
           "s_u1": dram["u1"], "s_v1": dram["v1"], "s_u2": dram["u2"],
           "s_v2": dram["v2"]}

    sb = {}
    base0 = (nc._sbuf_addr_for_side("left") + 31) // 32 * 32
    ptr = [base0]

    def A(name, shape, dtype=F32, at=None):
        size = int(np.prod(shape[1:])) * (2 if dtype == BF16 else 4)
        size = (size + 31) // 32 * 32
        if at is None:
            at = ptr[0]
            ptr[0] = at + size
        sb[name] = nc.alloc_sbuf_tensor_at(name, list(shape), dtype, offset=at)
        return at + size

    A("Wq", (128, 8 * D), BF16)
    A("Wk", (128, 8 * D), BF16)
    A("Wv", (128, 8 * D), BF16)
    A("Wbg", (128, 8 * 2 * H), BF16)
    u1b = ptr[0]
    A("QP", (128, 8 * tile_s), BF16, at=u1b)
    A("KA", (128, 8 * tile_s), BF16, at=u1b + 2 * 8 * tile_s)
    A("KM", (128, 8 * tile_s), BF16, at=u1b + 4 * 8 * tile_s)
    A("KT", (128, GR * H * 64), BF16, at=u1b + 6 * 8 * tile_s)
    A("RHS", (128, GR * H * 128), BF16, at=u1b + 6 * 8 * tile_s + 2 * GR * H * 64)
    A("WoS", (128, 8 * D), BF16, at=u1b + 2 * 32 * tile_s)
    A("H1", (128, 32 * tile_s), BF16, at=u1b)
    A("WSQ", (128, 2 * 4096), BF16, at=u1b + 2 * 32 * tile_s)
    A("U1", (128, 8 * RANK), BF16, at=u1b)
    A("V1", (128, 2 * FF), BF16, at=u1b + 2 * 8 * RANK)
    A("U2", (128, 32 * RANK), BF16, at=u1b + 2 * (8 * RANK + 2 * FF))
    A("V2", (128, 2 * D), BF16, at=u1b + 2 * (8 * RANK + 2 * FF + 32 * RANK))
    A("PF", (128, 2 * tile_s), BF16,
      at=u1b + 2 * (8 * RANK + 2 * FF + 32 * RANK + 2 * D))
    A("QF", (128, 2 * tile_s), BF16,
      at=u1b + 2 * (8 * RANK + 2 * FF + 32 * RANK + 2 * D + 2 * tile_s))
    A("HFB", (128, 2 * tile_s), BF16,
      at=u1b + 2 * (8 * RANK + 2 * FF + 32 * RANK + 2 * D + 4 * tile_s))
    ptr[0] = u1b + max(
        2 * 32 * tile_s + 2 * 2 * 4096,
        6 * 8 * tile_s + 2 * (GR * H * 64 + GR * H * 128),
        2 * (8 * RANK + 2 * FF + 32 * RANK + 2 * D + 6 * tile_s),
        2 * 8 * D)
    A("XF", (128, 8 * tile_s))
    A("X1F", (128, 8 * tile_s))
    u2b = ptr[0]
    A("SQS", (128, 8 * tile_s), BF16, at=u2b)
    A("OF", (128, 8 * tile_s), BF16, at=u2b)
    A("RT", (128, GR * D), F32, at=u2b)
    ptr[0] = u2b + 4 * GR * D
    opb_ = ptr[0]
    A("OPA", (128, 8 * tile_s), BF16, at=opb_)
    A("WUW", (128, H * 128), BF16, at=opb_)
    ptr[0] = opb_ + 2 * 8 * tile_s
    wsb = ptr[0]
    A("WL", (128, H * 64), BF16, at=wsb)
    A("WU", (128, H * 64), BF16, at=wsb + 2 * H * 64)
    A("WY0", (128, H * 128), BF16, at=wsb)
    A("WUmI", (128, H * 64), BF16, at=wsb + 2 * H * 128)
    A("WU2I", (128, H * 64), BF16, at=wsb + 2 * H * 128 + 2 * H * 64)
    ptr[0] = wsb + 4 * H * 128
    A("WPT", (128, H * 64), BF16)
    A("WRT0", (64, H * 64), BF16)
    A("WRT1", (64, H * 64), BF16)
    A("WOI0", (64, H * 64), BF16)
    A("AUG0", (128, H * 64), BF16)
    A("ST0", (128, H * 64), BF16)
    A("ST1", (128, H * 64), BF16)
    A("YTM", (128, GR * D), F32, at=wsb)
    eb = ptr[0]
    A("E1", (128, tile_s), BF16, at=eb)
    A("E2", (128, tile_s), BF16, at=eb + 2 * tile_s)
    A("E3", (128, tile_s), BF16, at=eb + 4 * tile_s)
    A("BCmu", (128, tile_s), BF16, at=eb)
    A("BCrs", (128, tile_s), BF16, at=eb + 2 * tile_s)
    ptr[0] = eb + 6 * tile_s
    for nm in ["BT", "GT", "EM", "EMM", "ERV", "RK", "CB1", "CB2", "L1T",
               "SSQ"]:
        A(nm, (128, GR * 16))
    A("EMDc", (128, NCH * 16))
    A("GCc", (128, NCH * 16))
    A("SCM", (128, 96))
    A("HMR", (48, tile_s), BF16)
    A("MSKL", (64, 512), BF16)
    A("MSKU", (64, 512), BF16)
    A("MSKP", (64, 512), BF16)
    A("IREP", (128, 512), BF16)
    A("CUMS", (128, 256))
    A("ESEL", (48, 3 * 8 * 128), BF16)
    A("ONE128", (128, 1), BF16)
    A("ONE1R", (1, 128), BF16)
    A("ONE1RF", (1, 128))
    A("ID128F", (128, 128))
    A("FVAL", (128, 16))
    A("G1C", (128, 8))
    A("G2C", (128, 8))
    A("B1LCB", (128, 8), BF16)
    A("B2LCB", (128, 8), BF16)
    A("BQ", (128, 8))
    A("BKc", (128, 8))
    A("B1AC", (128, 32))
    A("B2AC", (128, 8))
    A("BOCC", (128, 8))
    A("B1FC", (128, 32))
    A("GB1", (128, 32))
    A("BKR", (1, D), BF16)
    A("BVR", (1, D), BF16)
    A("BBGR", (1, 2 * H), BF16)
    A("B2FRB", (1, D), BF16)
    lrb = ptr[0]
    A("LR", (1, 3 * tile_s), at=lrb)
    A("CTMP", (64, 512), at=lrb)
    ptr[0] = lrb + 12 * tile_s
    A("STAT", (128, 24))
    A("MV", (128, 8))
    assert ptr[0] <= 229376, f"SBUF over budget: {ptr[0] - base0}"

    ps = {}
    for nm in ["pA", "pB", "pC", "pD", "pE", "pF", "pG", "pH"]:
        ps[nm] = nc.alloc_psum_tensor(nm, [128, 512], F32)
    PB = [ps[n] for n in ["pA", "pB", "pC", "pD", "pE", "pF", "pG", "pH"]]

    for i, v in enumerate([1.0 / D, EPS, 0.5, -0.5, -1.0, 1e-6, 1.0, 0.125,
                           2.0, 1e-30, 1.0 / 64, 0.0]):
        nc.const_aps.register(F32, float(v), sb["FVAL"].ap()[:, i:i + 1])

    sch = Sched()
    mm, op, dma = sch.mm, sch.op, sch.dma

    def T(name):
        return sb[name].ap()

    def Tb(name, p0, np_, c0, nc_):
        return sb[name].ap()[p0:p0 + np_, c0:c0 + nc_]

    DUMP_OUTS = []

    def dump(label, bar):
        for (oname, sname, box) in dumps.get(label, ()):
            p0, p1, c0, c1 = box
            od = nc.dram_tensor(oname, [p1 - p0, c1 - c0], sb[sname].dtype,
                                kind="ExternalOutput").ap()
            DUMP_OUTS.append(oname)
            dma("q", lambda e, od=od, s=sname, b=box: e.dma_start(
                out=od, in_=sb[s].ap()[b[0]:b[1], b[2]:b[3]]), deps=bar)

    for (dst, src) in [("MSKL", "cm_tril_s"), ("MSKU", "cm_triu_s"),
                       ("MSKP", "cm_triu_i"), ("IREP", "cm_irep"),
                       ("ESEL", "c_esel"),
                       ("ONE128", "c_ones128"), ("ONE1R", "c_ones1r"),
                       ("B1LCB", "b1lcb"), ("B2LCB", "b2lcb"),
                       ("AUG0", "c_auginit"), ("B2FRB", "b2fr")]:
        dma("g", lambda e, d=dst, s=src: e.dma_start(out=T(d), in_=dram[s]))
    for (dst, src) in [("ONE1RF", "c_ones1rf"), ("ID128F", "c_id128f"),
                       ("CUMS", "c_cums"),
                       ("FVAL", "c_fvals"), ("G1C", "g1c"), ("G2C", "g2c"),
                       ("B1AC", "b1ac"), ("B2AC", "b2ac"), ("BOCC", "bocc"),
                       ("B1FC", "b1fc")]:
        dma("q", lambda e, d=dst, s=src: e.dma_start(out=T(d), in_=dram[s]))
    for (dst, src, kc, oc) in [("Wq", "wq", 8, D), ("Wk", "wk", 8, D),
                               ("Wv", "wv", 8, D), ("Wbg", "wbg", 8, 2 * H)]:
        dma("g", lambda e, d=dst, s=src, kc=kc, oc=oc: e.dma_start(
            out=sb[d].ap()[:, 0:kc * oc].rearrange("p (c o) -> p c o", o=oc),
            in_=dram[s].rearrange("(c p) o -> p c o", p=128)))
    ld_bar = sch.barrier()

    def bias_cols(wname, bname, deps):
        evs = []
        for oc in range(8):
            last = None
            for kc in range(8):
                last = mm(lambda e, w=wname, kc=kc, oc=oc: e.matmul(
                    ps["pH"].ap()[0:128, oc:oc + 1],
                    sb[w].ap()[:, kc * D + oc * 128:kc * D + oc * 128 + 128],
                    T("B1LCB")[:, kc:kc + 1],
                    start=(kc == 0), stop=(kc == 7)), deps=deps)
            evs.append(last)
        return op("s", lambda e, b=bname: e.copy(
            out=T(b), in_=ps["pH"].ap()[0:128, 0:8]), deps=evs)

    tq = bias_cols("Wq", "BQ", ld_bar)
    tk = bias_cols("Wk", "BKc", ld_bar + [tq])

    def bias_rows(wname, bname, ocols, deps):
        last_ev = None
        for oh in range(max(1, ocols // 512)):
            w = min(512, ocols)
            last = None
            for kc in range(8):
                last = mm(lambda e, wn=wname, kc=kc, oh=oh, w=w, oc=ocols: e.matmul(
                    ps["pG"].ap()[0:1, 0:w], T("B1LCB")[:, kc:kc + 1],
                    sb[wn].ap()[:, kc * oc + oh * 512:kc * oc + oh * 512 + w],
                    start=(kc == 0), stop=(kc == 7)),
                    deps=deps + ([last_ev] if last_ev else []))
            last_ev = op("s", lambda e, b=bname, oh=oh, w=w: e.copy(
                out=sb[b].ap()[0:1, oh * 512:oh * 512 + w],
                in_=ps["pG"].ap()[0:1, 0:w]), deps=[last])
        return last_ev

    tkr = bias_rows("Wk", "BKR", D, ld_bar + [tk])
    tvr = bias_rows("Wv", "BVR", D, ld_bar + [tkr])
    tbr = bias_rows("Wbg", "BBGR", 2 * H, ld_bar + [tvr])

    scast_done = []
    init_bar = sch.barrier()
    prev_bar = init_bar + [tq, tk, tkr, tvr, tbr]

    def ln_operand(src, gcol, bar, bar_dc=None, of_free=None, ceng="g",
                   pgh_free=None):
        if bar_dc is None:
            tcl = [op("g", lambda e: e.tensor_copy(out=T("OPA"), in_=T(src)),
                      deps=bar)] * 8
            tsl = [op("g", lambda e: e.tensor_tensor(
                out=T("SQS"), in0=T(src), in1=T(src), op=ALU.mult),
                deps=bar)] * 8
        else:
            tcl = [op("g", lambda e, dc=dc: e.tensor_copy(
                out=sb["OPA"].ap()[:, dc * tile_s:(dc + 1) * tile_s],
                in_=sb[src].ap()[:, dc * tile_s:(dc + 1) * tile_s]),
                deps=bar_dc[dc]) for dc in range(8)]
            tsl = [op("g", lambda e, dc=dc: e.tensor_tensor(
                out=sb["SQS"].ap()[:, dc * tile_s:(dc + 1) * tile_s],
                in0=sb[src].ap()[:, dc * tile_s:(dc + 1) * tile_s],
                in1=sb[src].ap()[:, dc * tile_s:(dc + 1) * tile_s],
                op=ALU.mult),
                deps=bar_dc[dc] + ([of_free[dc]] if of_free else []))
                for dc in range(8)]
        tsq = tsl[-1]
        rowt = []
        for (srcn, pn, depl) in [("OPA", "pG", tcl), ("SQS", "pH", tsl)]:
            last = None
            for kc in range(8):
                last = mm(lambda e, s=srcn, kc=kc, pn=pn: e.matmul(
                    ps[pn].ap()[0:1, 0:tile_s], T("ONE128"),
                    sb[s].ap()[:, kc * tile_s:(kc + 1) * tile_s],
                    start=(kc == 0), stop=(kc == 7)),
                    deps=[depl[kc]] + ([last] if last else (pgh_free or [])))
            rowt.append(last)
        LR = sb["LR"].ap()
        tmu = op("s", lambda e: e.activation(
            out=LR[0:1, 0:tile_s], in_=ps["pG"].ap()[0:1, 0:tile_s],
            func=AF.Copy, scale=1.0 / D), deps=rowt)
        tm2 = op("v", lambda e: e.tensor_tensor(
            out=LR[0:1, 2 * tile_s:3 * tile_s], in0=LR[0:1, 0:tile_s],
            in1=LR[0:1, 0:tile_s], op=ALU.mult), deps=[tmu])
        tvar = op("v", lambda e: e.scalar_tensor_tensor(
            out=LR[0:1, 2 * tile_s:3 * tile_s],
            in0=ps["pH"].ap()[0:1, 0:tile_s], scalar=1.0 / D,
            in1=LR[0:1, 2 * tile_s:3 * tile_s],
            op0=ALU.mult, op1=ALU.subtract), deps=[tm2] + rowt)
        tln = op("s", lambda e: e.activation(
            out=LR[0:1, 2 * tile_s:3 * tile_s],
            in_=LR[0:1, 2 * tile_s:3 * tile_s], func=AF.Ln, bias=EPS),
            deps=[tvar])
        trs = op("s", lambda e: e.activation(
            out=LR[0:1, tile_s:2 * tile_s], in_=LR[0:1, 2 * tile_s:3 * tile_s],
            func=AF.Exp, scale=-0.5), deps=[tln])
        t1 = mm(lambda e: e.matmul(ps["pH"].ap()[:, 0:tile_s], T("ONE1RF"),
                                   LR[0:1, 0:tile_s], start=True, stop=True),
                deps=[tmu, tvar])
        t1e = op("s", lambda e: e.copy(out=T("BCmu"),
                                       in_=ps["pH"].ap()[:, 0:tile_s]), deps=[t1])
        t2 = mm(lambda e: e.matmul(ps["pH"].ap()[:, 0:tile_s], T("ONE1RF"),
                                   LR[0:1, tile_s:2 * tile_s],
                                   start=True, stop=True), deps=[trs, t1e])
        t2e = op("s", lambda e: e.copy(out=T("BCrs"),
                                       in_=ps["pH"].ap()[:, 0:tile_s]), deps=[t2])
        outs = []
        for dc in range(8):
            p1 = op("v", lambda e, dc=dc: e.tensor_tensor(
                out=sb["OPA"].ap()[:, dc * tile_s:(dc + 1) * tile_s],
                in0=sb[src].ap()[:, dc * tile_s:(dc + 1) * tile_s],
                in1=T("BCmu"), op=ALU.subtract),
                deps=[t1e, tsq] + rowt)
            if gcol is None:
                p2 = op("g", lambda e, dc=dc: e.tensor_tensor(
                    out=sb["OPA"].ap()[:, dc * tile_s:(dc + 1) * tile_s],
                    in0=sb["OPA"].ap()[:, dc * tile_s:(dc + 1) * tile_s],
                    in1=T("BCrs"), op=ALU.mult), deps=[p1, t2e])
            else:
                p2 = op("v", lambda e, dc=dc, g=gcol: e.scalar_tensor_tensor(
                    out=sb["OPA"].ap()[:, dc * tile_s:(dc + 1) * tile_s],
                    in0=sb["OPA"].ap()[:, dc * tile_s:(dc + 1) * tile_s],
                    scalar=sb[g].ap()[:, dc:dc + 1], in1=T("BCrs"),
                    op0=ALU.mult, op1=ALU.mult), deps=[p1, t2e])
            outs.append(p2)
        return outs

    def km_sl(name, h, c):
        hp, hh = h // 2, h % 2
        return sb[name].ap()[hh * 64:hh * 64 + 64,
                             hp * tile_s + c * 64:hp * tile_s + c * 64 + 64]

    def kt_sl(h, c):
        par, g = c % 2, c // 2
        return sb["KT"].ap()[par * 64:par * 64 + 64,
                             g * H * 64 + h * 64:g * H * 64 + h * 64 + 64]

    def rhs_sl(h, c):
        par, g = c % 2, c // 2
        return sb["RHS"].ap()[par * 64:par * 64 + 64,
                              g * H * 128 + h * 128:g * H * 128 + h * 128 + 128]

    def ws(name, h, w=64, c=0):
        pb = (c % 2) * 64
        return sb[name].ap()[pb:pb + 64, h * w:h * w + w]

    def ws0(name, h, w=64):
        return sb[name].ap()[0:64, h * w:h * w + w]

    def ws_par(name, par, w=64, c=0):
        pb = (c % 2) * 64
        return sb[name].ap()[pb:pb + 64, :].rearrange(
            "p (i r j) -> p i r j", r=2, j=w)[:, :, par, :]

    def ws_par0(name, par, w=64):
        return sb[name].ap()[0:64, :].rearrange(
            "p (i r j) -> p i r j", r=2, j=w)[:, :, par, :]

    def km_par(name, par, c):
        return sb[name].ap()[par * 64:par * 64 + 64, :].rearrange(
            "p (i j) -> p i j", j=tile_s)[:, :, c * 64:c * 64 + 64]

    def scol(name, h, c):
        par, g = c % 2, c // 2
        return sb[name].ap()[par * 64:par * 64 + 64, g * 16 + h:g * 16 + h + 1]

    def scol_par(name, par_h, c, n=64):
        v = sb[name].ap()[par_h * 64:par_h * 64 + 64, c * 16:c * 16 + 16] \
            .rearrange("p (i r) -> p i r", r=2)[:, :, par_h]
        return bc(v, n)

    prev_xf = None
    prev_aln = prev_bar
    prev_adc = [prev_bar] * 8
    prev_pgh = []
    prev_tso = []
    prev_fq = prev_bar
    for tile in range(n_tiles):
        b, st = tile // TPB, tile % TPB

        tx = dma("g", lambda e, b=b, st=st: e.dma_start(
            out=T("XF").rearrange("p (c t) -> p c t", t=tile_s),
            in_=dram["xfm"][b, st].rearrange("c p t -> p c t")),
            deps=(prev_xf if prev_xf is not None else prev_bar))
        bar0 = sch.barrier() + [tx]

        aln = ln_operand("XF", "G1C", prev_aln + [tx],
                         bar_dc=[prev_adc[dc] + [tx] for dc in range(8)],
                         ceng="v", pgh_free=prev_pgh)
        dual0 = [aln[-1]] + prev_aln
        dump(f"t{tile}_opa", sch.barrier())
        bar1 = sch.barrier()

        tbs, tes, tls = {}, {}, {}
        for g in range(GR):
            bpn = ["pG", "pH"][g % 2]
            bc0 = (g // 2) * 32
            bfree = [tbs[g - 2], tes[g - 2]] if g >= 2 else dual0
            lbg = None
            for kc in range(8):
                lbg = mm(lambda e, g=g, kc=kc, bpn=bpn, bc0=bc0: e.matmul(
                    ps[bpn].ap()[0:128, bc0:bc0 + 32],
                    sb["OPA"].ap()[:, kc * tile_s + g * 128:kc * tile_s + g * 128 + 128],
                    sb["Wbg"].ap()[:, kc * 32:kc * 32 + 32],
                    start=(kc == 0), stop=False), deps=bfree)
            lbg = mm(lambda e, g=g, bpn=bpn, bc0=bc0: e.matmul(
                ps[bpn].ap()[0:128, bc0:bc0 + 32], T("ONE1R"),
                T("BBGR")[0:1, 0:32], start=False, stop=True), deps=[lbg])
            gs = g * 16
            tbs[g] = op("s", lambda e, gs=gs, bpn=bpn, bc0=bc0: e.activation(
                out=T("BT")[:, gs:gs + 16],
                in_=ps[bpn].ap()[0:128, bc0:bc0 + 16],
                func=AF.Sigmoid), deps=[lbg])
            tes[g] = op("s", lambda e, gs=gs, bpn=bpn, bc0=bc0: e.activation(
                out=T("L1T")[:, gs:gs + 16],
                in_=ps[bpn].ap()[0:128, bc0 + 16:bc0 + 32],
                func=AF.Exp, scale=-1.0), deps=[lbg])
            tls[g] = op("s", lambda e, gs=gs: e.activation(
                out=T("L1T")[:, gs:gs + 16], in_=T("L1T")[:, gs:gs + 16],
                func=AF.Ln, bias=1.0), deps=[tes[g]])
        bg_all = [tbs[g] for g in range(GR)] + [tes[g] for g in range(GR)]
        def emit_kv(g, which):
            gs = g * 16
            pk0, pk1 = [("pA", "pB"), ("pC", "pD")][g % 2]
            kfree = (ktr_k[g - 2] + [tk2s[g - 2][0], tk2s[g - 2][1]]
                     if g >= 2 and which == "k" else dual0)
            vfree = (ktr_v[g - 1] if g >= 1 else
                     [t for gg in range(1, GR, 2) for t in cevs[gg]]) \
                if which == "v" else dual0
            arr = []
            for half in range(2):
                pk = pk0 if half == 0 else pk1
                pv = "pE" if half == 0 else "pF"
                lk = lv = None
                for kc in range(8):
                    if which == "k":
                        lk = mm(lambda e, g=g, kc=kc, half=half, pk=pk: e.matmul(
                            ps[pk].ap()[0:128, 0:512],
                            sb["OPA"].ap()[:, kc * tile_s + g * 128:kc * tile_s + g * 128 + 128],
                            sb["Wk"].ap()[:, kc * D + half * 512:kc * D + half * 512 + 512],
                            start=(kc == 0), stop=False), deps=kfree)
                    else:
                        lv = mm(lambda e, g=g, kc=kc, half=half, pv=pv: e.matmul(
                            ps[pv].ap()[0:128, 0:512],
                            sb["OPA"].ap()[:, kc * tile_s + g * 128:kc * tile_s + g * 128 + 128],
                            sb["Wv"].ap()[:, kc * D + half * 512:kc * D + half * 512 + 512],
                            start=(kc == 0), stop=False), deps=vfree)
                if which == "k":
                    arr.append(mm(lambda e, half=half, pk=pk: e.matmul(
                        ps[pk].ap()[0:128, 0:512], T("ONE1R"),
                        sb["BKR"].ap()[0:1, half * 512:half * 512 + 512],
                        start=False, stop=True), deps=[lk]))
                else:
                    arr.append(mm(lambda e, half=half, pv=pv: e.matmul(
                        ps[pv].ap()[0:128, 0:512], T("ONE1R"),
                        sb["BVR"].ap()[0:1, half * 512:half * 512 + 512],
                        start=False, stop=True), deps=[lv]))
            return arr

        karr0 = emit_kv(0, "k")
        garr, cbs, tts, thes, ktr_k, ktr_v = {}, {}, {}, {}, {}, {}
        tk2s, treds = {}, {}
        cevs = {}
        prev_par = None
        for g in range(GR):
            gs = g * 16
            cpn = ["pG", "pF"][g % 2]
            c0 = 128 + (g // 2) * 48 if g % 2 == 0 else (g // 2) * 48
            cev = []
            for par in range(2):
                pb = par * 64
                a1 = mm(lambda e, pb=pb, gs=gs, c0=c0, cpn=cpn: e.matmul(
                    ps[cpn].ap()[0:64, c0:c0 + 16],
                    sb["CUMS"].ap()[pb:pb + 64, 0:64],
                    sb["L1T"].ap()[pb:pb + 64, gs:gs + 16],
                    start=True, stop=True),
                    deps=[tls[g]] + ([prev_par] if prev_par else bg_all))
                a2 = mm(lambda e, pb=pb, gs=gs, c0=c0, cpn=cpn: e.matmul(
                    ps[cpn].ap()[0:64, c0 + 16:c0 + 32],
                    sb["CUMS"].ap()[pb:pb + 64, 64:128],
                    sb["L1T"].ap()[pb:pb + 64, gs:gs + 16],
                    start=True, stop=True), deps=[a1])
                a3 = mm(lambda e, pb=pb, gs=gs, c0=c0, cpn=cpn: e.matmul(
                    ps[cpn].ap()[0:64, c0 + 32:c0 + 48],
                    sb["CUMS"].ap()[pb:pb + 64, 128:192],
                    sb["L1T"].ap()[pb:pb + 64, gs:gs + 16],
                    start=True, stop=True), deps=[a2])
                e1 = op("s", lambda e, pb=pb, gs=gs, c0=c0, cpn=cpn: e.activation(
                    out=sb["GT"].ap()[pb:pb + 64, gs:gs + 16],
                    in_=ps[cpn].ap()[0:64, c0:c0 + 16], func=AF.Exp), deps=[a3])
                e2 = op("s", lambda e, pb=pb, gs=gs, c0=c0, cpn=cpn: e.activation(
                    out=sb["EM"].ap()[pb:pb + 64, gs:gs + 16],
                    in_=ps[cpn].ap()[0:64, c0 + 16:c0 + 32], func=AF.Exp),
                    deps=[a3])
                e3 = op("s", lambda e, pb=pb, gs=gs, c0=c0, cpn=cpn: e.activation(
                    out=sb["EMM"].ap()[pb:pb + 64, gs:gs + 16],
                    in_=ps[cpn].ap()[0:64, c0 + 16:c0 + 32], func=AF.Exp,
                    scale=-1.0), deps=[a3])
                e4 = op("s", lambda e, pb=pb, gs=gs, c0=c0, cpn=cpn: e.activation(
                    out=sb["ERV"].ap()[pb:pb + 64, gs:gs + 16],
                    in_=ps[cpn].ap()[0:64, c0 + 32:c0 + 48], func=AF.Exp),
                    deps=[a3])
                prev_par = e4
                cev += [e1, e2, e3, e4]
            cevs[g] = cev
        def emit_chain(g, karr):
            gs = g * 16
            pk0, pk1 = [("pA", "pB"), ("pC", "pD")][g % 2]
            if True:
                tk2a = op("s", lambda e, pk0=pk0: e.activation(
                    out=sb["SQS"].ap()[:, 0:512],
                    in_=ps[pk0].ap()[0:128, 0:512], func=AF.Square),
                    deps=karr + ([treds[g - 1]] if g >= 1 else []))
                tk2b = op("s", lambda e, pk1=pk1: e.activation(
                    out=sb["SQS"].ap()[:, 512:1024],
                    in_=ps[pk1].ap()[0:128, 0:512], func=AF.Square),
                    deps=karr + ([treds[g - 1]] if g >= 1 else []))
                tk2s[g] = (tk2a, tk2b)
                tred = op("v", lambda e, gs=gs: e.tensor_reduce(
                    out=T("SSQ")[:, gs:gs + 16].rearrange("p (a o) -> p a o", o=1),
                    in_=sb["SQS"].ap()[:, 0:1024].rearrange("p (h j) -> p h j", h=16),
                    axis=AX, op=ALU.add), deps=[tk2a, tk2b])
                treds[g] = tred
                tr1 = op("s", lambda e, gs=gs: e.activation(
                    out=T("RK")[:, gs:gs + 16], in_=T("SSQ")[:, gs:gs + 16],
                    func=AF.Ln, bias=1e-30), deps=[tred])
                tr2 = op("s", lambda e, gs=gs: e.activation(
                    out=T("RK")[:, gs:gs + 16], in_=T("RK")[:, gs:gs + 16],
                    func=AF.Exp, scale=0.5), deps=[tr1])
                tr3 = op("v", lambda e, gs=gs: e.tensor_scalar_add(
                    out=T("RK")[:, gs:gs + 16], in0=T("RK")[:, gs:gs + 16],
                    scalar1=1e-6), deps=[tr2])
                tr4 = op("v", lambda e, gs=gs: e.reciprocal(
                    out=T("RK")[:, gs:gs + 16], in_=T("RK")[:, gs:gs + 16]),
                    deps=[tr3])
                cev = cevs[g]
                sc0 = (g % 2) * 48
                cb = []
                cb.append(op("v", lambda e, gs=gs: e.tensor_tensor(
                    out=T("CB1")[:, gs:gs + 16], in0=T("ERV")[:, gs:gs + 16],
                    in1=T("RK")[:, gs:gs + 16], op=ALU.mult),
                    deps=cev + [tr4, tbs[g]]))
                for (o, i0, i1) in [("EMDc", "GT", "EMM"), ("GCc", "GT", "ERV")]:
                    for par in range(2):
                        c_ = 2 * g + par
                        pb = par * 64
                        for ob in range(2):
                            cb.append(op("v", lambda e, o=o, i0=i0, i1=i1, gs=gs,
                                         pb=pb, ob=ob, c_=c_: e.tensor_tensor(
                                out=sb[o].ap()[ob * 64:ob * 64 + 64,
                                               c_ * 16:c_ * 16 + 16],
                                in0=sb[i0].ap()[pb:pb + 64, gs:gs + 16],
                                in1=sb[i1].ap()[pb:pb + 64, gs:gs + 16],
                                op=ALU.mult), deps=cev))
                cb.append(op("v", lambda e, gs=gs: e.tensor_tensor(
                    out=T("CB2")[:, gs:gs + 16], in0=T("BT")[:, gs:gs + 16],
                    in1=T("GT")[:, gs:gs + 16], op=ALU.mult),
                    deps=cev + [tbs[g]]))
                cb.append(op("v", lambda e, gs=gs: e.tensor_tensor(
                    out=T("CB2")[:, gs:gs + 16], in0=T("CB2")[:, gs:gs + 16],
                    in1=T("RK")[:, gs:gs + 16], op=ALU.mult), deps=[tr4]))
                scm_free = [tts[g - 2]] if g >= 2 else []
                cb.append(op("v", lambda e, gs=gs, sc0=sc0: e.tensor_tensor(
                    out=T("SCM")[:, sc0:sc0 + 16], in0=T("RK")[:, gs:gs + 16],
                    in1=T("EMM")[:, gs:gs + 16], op=ALU.mult),
                    deps=cev + [tr4] + scm_free))
                cb.append(op("v", lambda e, gs=gs, sc0=sc0: e.tensor_tensor(
                    out=T("SCM")[:, sc0 + 16:sc0 + 32], in0=T("BT")[:, gs:gs + 16],
                    in1=T("EM")[:, gs:gs + 16], op=ALU.mult), deps=cev))
                cb.append(op("v", lambda e, gs=gs, sc0=sc0: e.tensor_tensor(
                    out=T("SCM")[:, sc0 + 16:sc0 + 32],
                    in0=T("SCM")[:, sc0 + 16:sc0 + 32],
                    in1=T("RK")[:, gs:gs + 16], op=ALU.mult), deps=[]))
                cb.append(op("v", lambda e, gs=gs, sc0=sc0: e.tensor_scalar_mul(
                    out=T("SCM")[:, sc0 + 32:sc0 + 48], in0=T("EM")[:, gs:gs + 16],
                    scalar1=0.125), deps=cev))
                cbs[g] = cb

        def emit_tail(gp):
            if True:
                gs = gp * 16
                sc0 = (gp % 2) * 48
                pk0, pk1 = [("pA", "pB"), ("pC", "pD")][gp % 2]
                karr, varr = garr[gp]
                tt = mm(lambda e, gp=gp, sc0=sc0: e.transpose(
                    out=ps["pH"].ap()[0:48, gp * 128:gp * 128 + 128],
                    in_=T("SCM")[:, sc0:sc0 + 48],
                    identity=T("ID128F")),
                    deps=cbs[gp] + ([thes[gp - 1]] if gp >= 1 else []))
                tts[gp] = tt
                the = op("s", lambda e, gp=gp: e.copy(
                    out=sb["HMR"].ap()[0:48, gp * 128:gp * 128 + 128],
                    in_=ps["pH"].ap()[0:48, gp * 128:gp * 128 + 128]),
                    deps=[tt])
                thes[gp] = the
                ktrhs_k, ktrhs_v = [], []
                for half in range(2):
                    pk = pk0 if half == 0 else pk1
                    pv = "pE" if half == 0 else "pF"
                    hv1 = bc(sb["CB1"].ap()[:, gs + half * 8:gs + half * 8 + 8]
                             .rearrange("p (h o) -> p h o", o=1)[:, :, 0], 64)
                    hv2 = bc(sb["CB2"].ap()[:, gs + half * 8:gs + half * 8 + 8]
                             .rearrange("p (h o) -> p h o", o=1)[:, :, 0], 64)
                    hvb = bc(sb["BT"].ap()[:, gs + half * 8:gs + half * 8 + 8]
                             .rearrange("p (h o) -> p h o", o=1)[:, :, 0], 64)
                    ktv = sb["KT"].ap()[:, gp * H * 64:gp * H * 64 + H * 64]                         .rearrange("p (i h j) -> p i h j", i=2, j=64)[:, half, :, :]
                    rv = sb["RHS"].ap()[:, gp * H * 128:(gp + 1) * H * 128]                         .rearrange("p (i h s j) -> p i h s j", i=2, s=2, j=64)
                    ktrhs_k.append(op("v", lambda e, pk=pk, hv1=hv1, ktv=ktv:
                                      e.tensor_tensor(
                        out=ktv,
                        in0=ps[pk].ap()[0:128, 0:512].rearrange(
                            "p (h j) -> p h j", h=8),
                        in1=hv1, op=ALU.mult), deps=cbs[gp] + karr + [the]))
                    ktrhs_k.append(op("v", lambda e, pk=pk, hv2=hv2, rv=rv,
                                      half=half: e.tensor_tensor(
                        out=rv[:, half, :, 1, :],
                        in0=ps[pk].ap()[0:128, 0:512].rearrange(
                            "p (h j) -> p h j", h=8),
                        in1=hv2, op=ALU.mult), deps=cbs[gp] + karr))
                    ktrhs_v.append(op("v", lambda e, pv=pv, hvb=hvb, rv=rv,
                                      half=half: e.tensor_tensor(
                        out=rv[:, half, :, 0, :],
                        in0=ps[pv].ap()[0:128, 0:512].rearrange(
                            "p (h j) -> p h j", h=8),
                        in1=hvb, op=ALU.mult), deps=varr + [tbs[gp]]))
                ktr_k[gp] = ktrhs_k
                ktr_v[gp] = ktrhs_v

        for g in range(GR + 1):
            if g < GR:
                karr = karr0 if g == 0 else emit_kv(g, "k")
            if g >= 1:
                emit_tail(g - 1)
            if g < GR:
                varr = emit_kv(g, "v")
                garr[g] = (karr, varr)
                emit_chain(g, karr)

        dump(f"t{tile}_dual", sch.barrier())
        bar2 = sch.barrier()

        et_hist, ee_hist, pj_hist = {}, {}, {}
        km0 = ktr_v[GR - 1] + [thes[GR - 1]]
        km1 = [aln[-1]] + prev_fq
        km2 = ktr_k[GR - 1] + ktr_k[GR - 2] + list(tk2s[GR - 1]) \
            + list(tk2s[GR - 2])
        for dc in range(8):
            pfold = ["pF", "pE"][dc % 2]
            pq, pk2 = [("pA", "pB"), ("pC", "pD")][dc % 2]
            etoks = []
            efree = et_hist.get(dc - 2, km0)
            esb_free = ee_hist.get(dc - 1, km1)
            for ty, en in [(0, "E1"), (1, "E2"), (2, "E3")]:
                te = mm(lambda e, ty=ty, dc=dc, pfold=pfold: e.matmul(
                    ps[pfold].ap()[:, 0:tile_s],
                    sb["ESEL"].ap()[:, (ty * 8 + dc) * 128:(ty * 8 + dc) * 128 + 128],
                    T("HMR"), start=True, stop=True), deps=efree + etoks)
                etoks.append(op("s", lambda e, en=en, pfold=pfold: e.copy(
                    out=T(en), in_=ps[pfold].ap()[:, 0:tile_s]),
                    deps=[te] + esb_free))
            lq = lk2 = None
            pjfree = pj_hist.get(dc - 2, km2)
            for kc in range(8):
                lq = mm(lambda e, dc=dc, kc=kc, pq=pq: e.matmul(
                    ps[pq].ap()[0:128, 0:tile_s],
                    sb["Wq"].ap()[:, kc * D + dc * 128:kc * D + dc * 128 + 128],
                    sb["OPA"].ap()[:, kc * tile_s:(kc + 1) * tile_s],
                    start=(kc == 0), stop=(kc == 7)), deps=pjfree)
                lk2 = mm(lambda e, dc=dc, kc=kc, pk2=pk2: e.matmul(
                    ps[pk2].ap()[0:128, 0:tile_s],
                    sb["Wk"].ap()[:, kc * D + dc * 128:kc * D + dc * 128 + 128],
                    sb["OPA"].ap()[:, kc * tile_s:(kc + 1) * tile_s],
                    start=(kc == 0), stop=(kc == 7)), deps=pjfree)
            e_q = op("v", lambda e, dc=dc, pq=pq: e.scalar_tensor_tensor(
                out=sb["QP"].ap()[:, dc * tile_s:(dc + 1) * tile_s],
                in0=ps[pq].ap()[0:128, 0:tile_s], scalar=T("BQ")[:, dc:dc + 1],
                in1=T("E3"), op0=ALU.add, op1=ALU.mult), deps=[lq] + etoks)
            e_a = op("v", lambda e, dc=dc, pk2=pk2: e.scalar_tensor_tensor(
                out=sb["KA"].ap()[:, dc * tile_s:(dc + 1) * tile_s],
                in0=ps[pk2].ap()[0:128, 0:tile_s], scalar=T("BKc")[:, dc:dc + 1],
                in1=T("E2"), op0=ALU.add, op1=ALU.mult), deps=[lk2] + etoks)
            e_m = op("v", lambda e, dc=dc, pk2=pk2: e.scalar_tensor_tensor(
                out=sb["KM"].ap()[:, dc * tile_s:(dc + 1) * tile_s],
                in0=ps[pk2].ap()[0:128, 0:tile_s], scalar=T("BKc")[:, dc:dc + 1],
                in1=T("E1"), op0=ALU.add, op1=ALU.mult), deps=[lk2] + etoks)
            et_hist[dc] = etoks
            ee_hist[dc] = [e_q, e_a, e_m]
            pj_hist[dc] = [e_q, e_a, e_m]

        dump(f"t{tile}_km", sch.barrier())
        bar3 = sch.barrier()

        import os
        bar3n = [ee_hist[7][-1], et_hist[7][-1]]
        if st == 0:
            z0 = op("v", lambda e: e.memset(T("ST0"), 0.0), deps=bar3n)
            bar3n = bar3n + [z0]

        GRP = [("pA", "pB"), ("pC", "pD"), ("pE", "pF"), ("pG", "pH")]

        def slot(h, f=64):
            return (h % 2, (h // 2) * f)

        def sps(c, h, f=64):
            bk, off = slot(h, f)
            return ps[GRP[c % 4][bk]].ap()[0:64, off:off + f]

        def spsA(c, h):
            bk = h % 2
            off = ((h % 8) // 2) * 128
            return ps[GRP[c % 4][bk]].ap()[0:64, off:off + 128]

        def bank(c, i):
            return ps[GRP[c % 4][i]].ap()

        W = {}

        def stage_mms(c, lhs, rhs, deps, f=64, heads=range(H), k=0):
            toks = []
            for h in heads:
                toks.append(mm(lambda e, h=h, c=c, f=f, lhs=lhs, rhs=rhs: e.matmul(
                    sps(c, h, f) if f == 64 else spsA(c, h),
                    lhs(h, c), rhs(h, c), start=True, stop=True), deps=deps))
            W[c].setdefault("mlast", {})[k] = toks[-1]
            return toks

        def ev2(c, eng, outfn, deps, maskn=None, addI=False):
            toks = []
            for bk in range(2):
                b3 = bank(c, bk)[0:64, 0:512].rearrange("p (i j) -> p i j", j=64)
                if maskn is not None:
                    toks.append(op(eng, lambda e, c=c, bk=bk, outfn=outfn,
                                   b3=b3, m=maskn: e.tensor_tensor(
                        out=outfn(c, bk), in0=b3,
                        in1=T(m).rearrange("p (i j) -> p i j", j=64),
                        op=ALU.mult), deps=deps))
                elif addI:
                    toks.append(op(eng, lambda e, c=c, bk=bk, outfn=outfn,
                                   b3=b3: e.tensor_tensor(
                        out=outfn(c, bk), in0=b3,
                        in1=sb["IREP"].ap()[(c % 2) * 64:(c % 2) * 64 + 64, :]
                            .rearrange("p (i j) -> p i j", j=64),
                        op=ALU.add), deps=deps))
                else:
                    toks.append(op(eng, lambda e, c=c, bk=bk, outfn=outfn,
                                   b3=b3: ecopy(e, outfn(c, bk), b3), deps=deps))
            return toks

        def wsc(name, c, bk, w=64):
            pbc = (c % 2) * 64
            return sb[name].ap()[pbc:pbc + 64, :].rearrange(
                "p (i r j) -> p i r j", r=2, j=w)[:, :, bk, :]

        def pws(c, k, dj=2):
            wv = W.get(c - dj)
            if not wv:
                return []
            t = wv.get("mlast", {}).get(k)
            return [t] if t else []

        def S_emit(c, k):
            w = W[c]
            mx = int(os.environ.get("B_MAXSTAGE", "99"))
            if k > mx:
                if "sev" not in w:
                    w["sev"] = w.get("ev%d" % mx, w["in"]) or w["in"]
                    w["oev"] = w["sev"]
                    w["last_t"] = sch.barrier()[0] if sch.barrier() else None
                return
            pbc = (c % 2) * 64
            par = c % 2
            if k == 1:
                bank_free = []
                if (c - 4) in W and "sev" in W[c - 4]:
                    bank_free = W[c - 4]["sev"] + W[c - 4]["oev"]
                m = stage_mms(c, lambda h, c: km_sl("KA", h, c),
                              lambda h, c: km_sl("KM", h, c),
                              w["in"] + bank_free, 64, k=1)
                w["ev1"] = ev2(c, "v", lambda c, bk: wsc("WL", c, bk),
                               m + pws(c, 4), maskn="MSKL")
            elif k == 2:
                m = stage_mms(c, lambda h, c: km_sl("KM", h, c),
                              lambda h, c: km_sl("KA", h, c), w["ev1"], 64, k=2)
                w["ev2"] = ev2(c, "v", lambda c, bk: wsc("WU", c, bk),
                               m + pws(c, 4), maskn="MSKU")
                w["ev2b"] = [op("g", lambda e, c=c, bk=bk: e.tensor_tensor(
                    out=wsc("WUmI", c, bk),
                    in0=sb["IREP"].ap()[(c % 2) * 64:(c % 2) * 64 + 64, :],
                    in1=wsc("WU", c, bk), op=ALU.subtract),
                    deps=w["ev2"] + pws(c, 8)) for bk in range(2)]
            elif k == 3:
                m = stage_mms(c, lambda h, c: km_sl("KM", h, c),
                              lambda h, c: km_sl("QP", h, c), w["ev78"], 64, k=3)
                w["ev3"] = ev2(c, "v", lambda c, bk: wsc("WPT", c, bk),
                               m + pws(c, 12), maskn="MSKP")
                wrtn = "WRT0" if par == 0 else "WRT1"
                w["wrt"] = wrtn
                cm1p = W[c - 2].get("cm1last") if (c - 2) in W else None
                w["ev3b"] = [op("g", lambda e, ph=ph, c=c, wrtn=wrtn: e.tensor_tensor(
                    out=ws_par0(wrtn, ph), in0=km_par("QP", ph, c),
                    in1=scol_par("EMDc", ph, c), op=ALU.mult),
                    deps=w["in"] + ([cm1p] if cm1p else [])) for ph in range(2)]
            elif k == 4:
                m = stage_mms(c, lambda h, c: ws("WL", h, c=c),
                              lambda h, c: ws("WU", h, c=c),
                              w["ev1"] + w["ev2"] + w["ev2b"], 64, k=4)
                w["ev4"] = ev2(c, "v", lambda c, bk: wsc("WU2I", c, bk),
                               m + pws(c, 6), addI=True)
            elif k in (5, 6):
                heads = range(0, 8) if k == 5 else range(8, 16)
                m = stage_mms(c, lambda h, c: ws("WU2I", h, c=c),
                              lambda h, c: rhs_sl(h, c),
                              w["ev4"] + (w.get("ev5h") or []), 128, heads, k=6)
                evs = []
                h0 = 0 if k == 5 else 4
                for bk in range(2):
                    ov = sb["WY0"].ap()[(c % 2) * 64:(c % 2) * 64 + 64, :] \
                        .rearrange("p (i r j) -> p i r j", r=2, j=128)[:, h0:h0 + 4, bk, :]
                    evs.append(op("s", lambda e, c=c, bk=bk, ov=ov: e.copy(
                        out=ov, in_=bank(c, bk)[0:64, 0:512]
                        .rearrange("p (i j) -> p i j", j=128)), deps=m + pws(c, 8)))
                w["ev5h"] = evs if k == 5 else (w["ev5h"] + evs)
                if k == 6:
                    w["ev56"] = w["ev5h"]
            elif k in (7, 8):
                heads = range(0, 8) if k == 7 else range(8, 16)
                m = stage_mms(c, lambda h, c: ws("WUmI", h, c=c),
                              lambda h, c: ws("WY0", h, 128, c=c),
                              w["ev56"] + (w.get("ev7h") or []), 128, heads, k=8)
                evs = []
                h0 = 0 if k == 7 else 4
                for bk in range(2):
                    ov = sb["WUW"].ap()[(c % 2) * 64:(c % 2) * 64 + 64, :] \
                        .rearrange("p (i r j) -> p i r j", r=2, j=128)[:, h0:h0 + 4, bk, :]
                    evs.append(op("s", lambda e, c=c, bk=bk, ov=ov: ecopy(
                        e, ov, bank(c, bk)[0:64, 0:512]
                        .rearrange("p (i j) -> p i j", j=128)), deps=m + pws(c, 12)))
                w["ev7h"] = evs if k == 7 else (w["ev7h"] + evs)
                if k == 8:
                    w["ev78"] = w["ev7h"]
            elif k == 9:
                m = stage_mms(c, lambda h, c, pbc=pbc: sb["WUW"].ap()[
                    pbc:pbc + 64, h * 128 + 64:h * 128 + 128],
                    lambda h, c: kt_sl(h, c), w["ev3"] + w["ev78"], 64, k=9)
                augn = "AUG0"
                w["aug"] = augn
                pv = [W[c - 1]["cm2last"]] if (c - 1) in W and "cm2last" in W[c - 1] else []
                w["ev9"] = ev2(c, "s", lambda c, bk, augn=augn:
                               sb[augn].ap()[0:64, :].rearrange(
                                   "p (i r j) -> p i r j", r=2, j=64)[:, :, bk, :],
                               m + pv)
            elif k == 10:
                m = stage_mms(c, lambda h, c, pbc=pbc: sb["WUW"].ap()[
                    pbc:pbc + 64, h * 128 + 64:h * 128 + 128],
                    lambda h, c: ws("WPT", h, c=c), w["ev9"], 64, k=10)
                wrtn = w["wrt"]
                w["ev10"] = [op("v", lambda e, c=c, bk=bk, wrtn=wrtn: e.tensor_tensor(
                    out=ws_par0(wrtn, bk), in0=ws_par0(wrtn, bk),
                    in1=bank(c, bk)[0:64, 0:512].rearrange("p (i j) -> p i j", j=64),
                    op=ALU.subtract),
                    deps=m + w["ev3b"]) for bk in range(2)]
            elif k == 11:
                stA = w["stA"]
                m = stage_mms(c, lambda h, c: kt_sl(h, c),
                              lambda h, c, pbc=pbc: sb["WUW"].ap()[
                                  pbc:pbc + 64, h * 128:h * 128 + 64],
                              w["ev10"], 64, k=11)
                cm2p = [W[c - 2]["cm2last"]] if (c - 2) in W and "cm2last" in W[c - 2] else []
                w["ev11"] = ev2(c, "s", lambda c, bk, stA=stA:
                                sb[stA].ap()[64:128, :].rearrange(
                                    "p (i r j) -> p i r j", r=2, j=64)[:, :, bk, :],
                                m + cm2p)
            elif k == 12:
                m = stage_mms(c, lambda h, c, pbc=pbc: sb["WUW"].ap()[
                    pbc:pbc + 64, h * 128:h * 128 + 64],
                    lambda h, c: ws("WPT", h, c=c), w["ev11"], 64, k=12)
                woin = "WOI0"
                w["woi"] = woin
                pvo = W[c - 1].get("oev", []) if (c - 1) in W else []
                w["ev12"] = ev2(c, "s", lambda c, bk, woin=woin:
                                ws_par0(woin, bk), m + pvo)
            elif k == 13 and os.environ.get("B_NOCHAIN"):
                w["sev"] = w["ev12"]
                w["oev"] = w["ev12"]
                w["last_t"] = w["ev12"][-1] if w["ev12"] else None
                w["last_t"] = w["in"][0] if w["in"] else None
                import types
                w["last_t"] = sch.barrier()[0]
            elif k == 13:
                stA, stB = w["stA"], w["stB"]
                wrtn, augn, woin = w["wrt"], w["aug"], w["woi"]
                sdep = W[c - 1]["sev"] if (c - 1) in W else w["in"]
                cm1 = []
                for h in range(H):
                    pn = GRP[c % 4][h % 2]
                    off = (h // 2) * 64
                    cm1.append(mm(lambda e, h=h, pn=pn, off=off, stA=stA,
                                  wrtn=wrtn: e.matmul(
                        ps[pn].ap()[0:64, off:off + 64],
                        sb[stA].ap()[0:64, h * 64:h * 64 + 64],
                        sb[wrtn].ap()[0:64, h * 64:h * 64 + 64],
                        start=True, stop=True),
                        deps=w["ev12"] + w["ev10"] + sdep))
                oev = []
                for ph in range(2):
                    pn = GRP[c % 4][ph]
                    ofv = sb["OF"].ap()[ph * 64:ph * 64 + 64, :].rearrange(
                        "p (i j) -> p i j", j=tile_s)[:, :, c * 64:c * 64 + 64]
                    oev.append(op("v", lambda e, pn=pn, ofv=ofv, ph=ph,
                                  woin=woin: e.tensor_tensor(
                        out=ofv,
                        in0=ps[pn].ap()[0:64, 0:512].rearrange(
                            "p (i j) -> p i j", j=64),
                        in1=ws_par0(woin, ph), op=ALU.add), deps=cm1))
                cm2 = []
                for h in range(H):
                    pn = GRP[c % 4][h % 2]
                    off = (h // 2) * 64
                    cm2.append(mm(lambda e, h=h, pn=pn, off=off, stA=stA,
                                  augn=augn: e.matmul(
                        ps[pn].ap()[0:64, off:off + 64],
                        sb[augn].ap()[:, h * 64:h * 64 + 64],
                        sb[stA].ap()[:, h * 64:h * 64 + 64],
                        start=True, stop=True),
                        deps=oev + w["ev9"] + w["ev11"]))
                sev = []
                for ph in range(2):
                    pn = GRP[c % 4][ph]
                    t1 = op("v", lambda e, ph=ph, c=c, stA=stA: e.tensor_tensor(
                        out=sb["CTMP"].ap()[0:64, 0:512].rearrange(
                            "p (i j) -> p i j", j=64),
                        in0=sb[stA].ap()[0:64, :].rearrange(
                            "p (i r j) -> p i r j", r=2, j=64)[:, :, ph, :],
                        in1=bc(sb["GCc"].ap()[0:64, c * 16:c * 16 + 16]
                               .rearrange("p (i r) -> p i r", r=2)[:, :, ph], 64),
                        op=ALU.mult), deps=cm2)
                    t2 = op("v", lambda e, ph=ph, pn=pn, stB=stB: e.tensor_tensor(
                        out=sb[stB].ap()[0:64, :].rearrange(
                            "p (i r j) -> p i r j", r=2, j=64)[:, :, ph, :],
                        in0=sb["CTMP"].ap()[0:64, 0:512].rearrange(
                            "p (i j) -> p i j", j=64),
                        in1=ps[pn].ap()[0:64, 0:512].rearrange(
                            "p (i j) -> p i j", j=64),
                        op=ALU.subtract), deps=[t1] + cm2)
                    sev.append(t2)
                w["sev"] = sev
                w["oev"] = oev
                w["cm1last"] = cm1[-1]
                w["cm2last"] = cm2[-1]
                w["last_t"] = cm2[-1]

        for c in range(NCH):
            W[c] = {"in": bar3n,
                    "stA": "ST0" if c % 2 == 0 else "ST1",
                    "stB": "ST1" if c % 2 == 0 else "ST0"}
        if os.environ.get("B_SERIAL"):
            for c in range(NCH):
                for k in range(1, 14):
                    S_emit(c, k)
        else:
            SKEW = 4
            ORD = {1: 1, 2: 2, 3: 4, 4: 5, 5: 6, 6: 7, 7: 8, 8: 3,
                   9: 9, 10: 10, 11: 11, 12: 12, 13: 13}
            for Tt in range(1, SKEW * (NCH - 1) + 13 + 1):
                for c in range(NCH):
                    pos = Tt - SKEW * c
                    if 1 <= pos <= 13:
                        S_emit(c, ORD[pos])
        barB_toks = [W[c]["sev"][-1] for c in range(NCH)] \
            + [W[c]["oev"][-1] for c in range(NCH)]
        dump(f"t{tile}_B", sch.barrier())
        barB = sch.barrier()

        two = dma("g", lambda e: e.dma_start(
            out=sb["WoS"].ap()[:, 0:8 * D].rearrange("p (c o) -> p c o", o=D),
            in_=scr["s_wo"].rearrange(
                "(c p) o -> p c o", p=128)),
            deps=[W[NCH - 2]["mlast"].get(6), W[NCH - 1]["mlast"].get(6)])
        cprev = barB_toks + [two]
        cev = []
        wo_last_kc = {}
        for odc in range(8):
            pn = ["pA", "pB"][odc % 2]
            last = None
            for kc in range(8):
                last = mm(lambda e, odc=odc, kc=kc, pn=pn: e.matmul(
                    ps[pn].ap()[0:128, 0:tile_s],
                    sb["WoS"].ap()[:, kc * D + odc * 128:kc * D + odc * 128 + 128],
                    sb["OF"].ap()[:, kc * tile_s:(kc + 1) * tile_s],
                    start=(kc == 0), stop=(kc == 7)), deps=cprev + cev[-1:])
                wo_last_kc[kc] = last
            ev = op("v", lambda e, odc=odc, pn=pn: e.scalar_tensor_tensor(
                out=sb["X1F"].ap()[:, odc * tile_s:(odc + 1) * tile_s],
                in0=ps[pn].ap()[0:128, 0:tile_s], scalar=T("BOCC")[:, odc:odc + 1],
                in1=sb["XF"].ap()[:, odc * tile_s:(odc + 1) * tile_s],
                op0=ALU.add, op1=ALU.add), deps=[last] + prev_tso)
            cev.append(ev)
        dump(f"t{tile}_C", sch.barrier())
        barC = sch.barrier()

        dln = ln_operand("X1F", "G2C", barC,
                         bar_dc=[[cev[dc]] for dc in range(8)],
                         of_free=[wo_last_kc[dc] for dc in range(8)])
        barD0 = sch.barrier()
        h1ev = []
        wprev = None
        gb_done = []
        for q in range(8):
            buf = q % 2
            tw = dma("g", lambda e, q=q, buf=buf: e.dma_start(
                out=sb["WSQ"].ap()[:, buf * 4096:buf * 4096 + 4096]
                    .rearrange("p (c o) -> p c o", o=512),
                in_=scr["s_w1"][:, q * 512:(q + 1) * 512]
                    .rearrange("(c p) o -> p c o", p=128)),
                deps=(barB if q < 2 else [h1ev[(q - 2) * 4 + 3]]),
                sem="dg" if q % 2 == 0 else "dg2")
            if tile == 0:
                gtoks = []
                for fo in range(4):
                    last = None
                    for kc in range(8):
                        last = mm(lambda e, buf=buf, kc=kc, fo=fo: e.matmul(
                            ps["pH"].ap()[0:128, fo:fo + 1],
                            sb["WSQ"].ap()[:, buf * 4096 + kc * 512 + fo * 128:
                                           buf * 4096 + kc * 512 + fo * 128 + 128],
                            T("B2LCB")[:, kc:kc + 1],
                            start=(kc == 0), stop=(kc == 7)),
                            deps=[tw] + barD0)
                    gtoks.append(last)
                gb_done.append(op("v", lambda e, q=q: e.tensor_tensor(
                    out=T("GB1")[:, q * 4:q * 4 + 4],
                    in0=ps["pH"].ap()[0:128, 0:4],
                    in1=T("B1AC")[:, q * 4:q * 4 + 4],
                    op=ALU.add), deps=gtoks))
            for fo in range(4):
                pn = ["pA", "pB", "pC", "pD"][fo % 4]
                last = None
                for kc in range(8):
                    last = mm(lambda e, buf=buf, kc=kc, fo=fo, pn=pn: e.matmul(
                        ps[pn].ap()[0:128, 0:tile_s],
                        sb["WSQ"].ap()[:, buf * 4096 + kc * 512 + fo * 128:
                                       buf * 4096 + kc * 512 + fo * 128 + 128],
                        sb["OPA"].ap()[:, kc * tile_s:(kc + 1) * tile_s],
                        start=(kc == 0), stop=(kc == 7)),
                        deps=[tw, dln[kc]] + gb_done + h1ev[-4:])
                h1ev.append(op("s", lambda e, q=q, fo=fo, pn=pn: e.activation(
                    out=sb["H1"].ap()[:, (q * 4 + fo) * tile_s:(q * 4 + fo + 1) * tile_s],
                    in_=ps[pn].ap()[0:128, 0:tile_s], func=AF.Gelu,
                    bias=T("GB1")[:, q * 4 + fo:q * 4 + fo + 1], scale=1.0),
                    deps=[last]))
        barD1 = sch.barrier()
        wprev = None
        w2mm = {odc: [] for odc in range(8)}
        for q in range(8):
            buf = q % 2
            tw = dma("g", lambda e, q=q, buf=buf: e.dma_start(
                out=sb["WSQ"].ap()[:, buf * 4096:buf * 4096 + 4096]
                    .rearrange("p (c o) -> p c o", o=1024),
                in_=scr["s_w2"][q * 512:(q + 1) * 512, :]
                    .rearrange("(c p) o -> p c o", p=128)),
                deps=([h1ev[(6 * 4) + 3]] if q == 0 else [h1ev[-1]]
                      if q == 1 else [w2mm[7][(q - 2) * 4 + 3]]),
                sem="dg" if q % 2 == 0 else "dg2")
            for odc in range(8):
                for fc in range(4):
                    w2mm[odc].append(mm(lambda e, q=q, buf=buf, odc=odc, fc=fc: e.matmul(
                        PB[odc].ap()[0:128, 0:tile_s],
                        sb["WSQ"].ap()[:, buf * 4096 + fc * 1024 + odc * 128:
                                       buf * 4096 + fc * 1024 + odc * 128 + 128],
                        sb["H1"].ap()[:, (q * 4 + fc) * tile_s:(q * 4 + fc + 1) * tile_s],
                        start=(q == 0 and fc == 0), stop=(q == 7 and fc == 3)),
                        deps=[tw] + (h1ev[-1:] if q == 0 else [])))
        rev = []
        for odc in range(8):
            t1 = op("v", lambda e, odc=odc: e.scalar_tensor_tensor(
                out=sb["RT"].ap()[:, odc * tile_s:(odc + 1) * tile_s],
                in0=PB[odc].ap()[0:128, 0:tile_s], scalar=T("B2AC")[:, odc:odc + 1],
                in1=sb["X1F"].ap()[:, odc * tile_s:(odc + 1) * tile_s],
                op0=ALU.add, op1=ALU.add), deps=[w2mm[odc][-1]])
            t2 = op("g", lambda e, odc=odc: e.tensor_tensor(
                out=sb["X1F"].ap()[:, odc * tile_s:(odc + 1) * tile_s],
                in0=sb["RT"].ap()[:, odc * tile_s:(odc + 1) * tile_s],
                in1=sb["XF"].ap()[:, odc * tile_s:(odc + 1) * tile_s],
                op=ALU.add), deps=[t1])
            rev.append(t2)
        prev_xf = rev
        dump(f"t{tile}_D", sch.barrier())
        barD = sch.barrier()

        ffn_toks = []
        for (dst, srcn, kc, oc) in [("U1", "s_u1", 8, RANK), ("V1", "s_v1", 2, FF),
                                    ("U2", "s_u2", 32, RANK), ("V2", "s_v2", 2, D)]:
            ffn_toks.append(dma("g", lambda e, d=dst, s=srcn, kc=kc, oc=oc: e.dma_start(
                out=sb[d].ap()[:, 0:kc * oc].rearrange("p (c o) -> p c o", o=oc),
                in_=scr[s].rearrange("(c p) o -> p c o", p=128)),
                deps=[w2mm[o][-1] for o in range(8)] + ffn_toks[-1:]))
        tev = []
        for g in range(GR):
            for dc in range(8):
                pn = ["pE", "pF", "pG", "pH"][(g * 8 + dc) % 4]
                tt = mm(lambda e, g=g, dc=dc, pn=pn: e.matmul(
                    ps[pn].ap()[0:128, 0:128],
                    sb["X1F"].ap()[:, dc * tile_s + g * 128:dc * tile_s + g * 128 + 128],
                    T("ID128F"), start=True, stop=True, is_transpose=True),
                    deps=[rev[dc], rev[2 * g], rev[2 * g + 1],
                          rev[4], rev[5], rev[6], rev[7]] + tev[-4:])
                tev.append(op("s", lambda e, g=g, dc=dc, pn=pn: e.copy(
                    out=sb["RT"].ap()[:, g * D + dc * 128:g * D + dc * 128 + 128],
                    in_=ps[pn].ap()[0:128, 0:128]), deps=[tt]))
        yev = []
        for g in range(GR):
            mo = (g % 2) * 4
            so_ = (g % 2) * 12
            tev_g = tev[g * 8:g * 8 + 8]
            s1 = op("v", lambda e, g=g, so_=so_: e.bn_stats(
                out=T("STAT")[:, so_:so_ + 6],
                in_=sb["RT"].ap()[:, g * D:g * D + 512]),
                deps=tev_g)
            s2 = op("v", lambda e, g=g, so_=so_: e.bn_stats(
                out=T("STAT")[:, so_ + 6:so_ + 12],
                in_=sb["RT"].ap()[:, g * D + 512:g * D + D]),
                deps=tev_g + [s1])
            s3 = op("v", lambda e, mo=mo, so_=so_: e.bn_aggr(
                out=T("MV")[:, mo:mo + 2], in_=T("STAT")[:, so_:so_ + 12]),
                deps=[s1, s2])
            s4 = op("s", lambda e, mo=mo: e.activation(
                out=T("MV")[:, mo + 2:mo + 3], in_=T("MV")[:, mo + 1:mo + 2],
                func=AF.Ln, bias=EPS), deps=[s3])
            s5 = op("s", lambda e, mo=mo: e.activation(
                out=T("MV")[:, mo + 2:mo + 3], in_=T("MV")[:, mo + 2:mo + 3],
                func=AF.Exp, scale=-0.5), deps=[s4])
            yev.append(op("v", lambda e, g=g, mo=mo: e.tensor_scalar(
                out=sb["YTM"].ap()[:, g * D:g * D + D],
                in0=sb["RT"].ap()[:, g * D:g * D + D],
                scalar1=T("MV")[:, mo:mo + 1], scalar2=T("MV")[:, mo + 2:mo + 3],
                op0=ALU.subtract, op1=ALU.mult), deps=[s5]))
        dump(f"t{tile}_E", sch.barrier())
        barE = sch.barrier()

        flnb = [h1ev[-1], yev[0], yev[1], rev[6], rev[7]]
        fln = ln_operand("X1F", None, [h1ev[-1]], bar_dc=[flnb] * 8)
        barF0 = sch.barrier()
        pfe = []
        for ob in range(2):
            last = None
            for kc in range(8):
                last = mm(lambda e, ob=ob, kc=kc: e.matmul(
                    ps["pA"].ap()[0:128, 0:tile_s],
                    sb["U1"].ap()[:, kc * RANK + ob * 128:kc * RANK + ob * 128 + 128],
                    sb["OPA"].ap()[:, kc * tile_s:(kc + 1) * tile_s],
                    start=(kc == 0), stop=(kc == 7)),
                    deps=fln + [ffn_toks[0]] + h1ev[-2:] + pfe)
            pfe.append(op("s", lambda e, ob=ob: e.copy(
                out=sb["PF"].ap()[:, ob * tile_s:(ob + 1) * tile_s],
                in_=ps["pA"].ap()[0:128, 0:tile_s]), deps=[last]))
        qfmm = []
        hprev = []
        def hbuf(fo):
            b = fo % 3
            if b < 2:
                return sb["HFB"].ap()[:, b * tile_s:(b + 1) * tile_s]
            return T("E3")
        for fo in range(32):
            pn = ["pA", "pB", "pG", "pH"][fo % 4]
            last = None
            for rc in range(2):
                last = mm(lambda e, fo=fo, rc=rc, pn=pn: e.matmul(
                    ps[pn].ap()[0:128, 0:tile_s],
                    sb["V1"].ap()[:, rc * FF + fo * 128:rc * FF + fo * 128 + 128],
                    sb["PF"].ap()[:, rc * tile_s:(rc + 1) * tile_s],
                    start=(rc == 0), stop=(rc == 1)),
                    deps=pfe + [ffn_toks[1]] + hprev[-4:])
            hg = op("s", lambda e, fo=fo, pn=pn: e.activation(
                out=hbuf(fo),
                in_=ps[pn].ap()[0:128, 0:tile_s], func=AF.Gelu_apprx_tanh,
                bias=T("B1FC")[:, fo:fo + 1], scale=1.0),
                deps=[last] + qfmm[-6:])
            hprev.append(hg)
            for qb in range(2):
                qfmm.append(mm(lambda e, fo=fo, qb=qb: e.matmul(
                    ps["pC" if qb == 0 else "pD"].ap()[0:128, 0:tile_s],
                    sb["U2"].ap()[:, fo * RANK + qb * 128:fo * RANK + qb * 128 + 128],
                    hbuf(fo),
                    start=(fo == 0), stop=(fo == 31)),
                    deps=[hg, ffn_toks[2]]))
        qfe = [op("s", lambda e, qb=qb: e.copy(
            out=sb["QF"].ap()[:, qb * tile_s:(qb + 1) * tile_s],
            in_=ps["pC" if qb == 0 else "pD"].ap()[0:128, 0:tile_s]),
            deps=qfmm) for qb in range(2)]
        fev = []
        for g in range(GR):
            for dh in range(2):
                pn = ["pE", "pF"][dh]
                last = None
                for rc in range(2):
                    last = mm(lambda e, g=g, dh=dh, rc=rc, pn=pn: e.matmul(
                        ps[pn].ap()[0:128, 0:512],
                        sb["QF"].ap()[:, rc * tile_s + g * 128:rc * tile_s + g * 128 + 128],
                        sb["V2"].ap()[:, rc * D + dh * 512:rc * D + dh * 512 + 512],
                        start=(rc == 0), stop=False),
                        deps=qfe + [ffn_toks[3]] + fev[-2:])
                last = mm(lambda e, dh=dh, pn=pn: e.matmul(
                    ps[pn].ap()[0:128, 0:512], T("ONE1R"),
                    T("B2FRB")[0:1, dh * 512:dh * 512 + 512],
                    start=False, stop=True), deps=[last])
                fev.append(op("v", lambda e, g=g, dh=dh, pn=pn: e.tensor_tensor(
                    out=sb["RT"].ap()[:, g * D + dh * 512:g * D + dh * 512 + 512],
                    in0=ps[pn].ap()[0:128, 0:512],
                    in1=sb["YTM"].ap()[:, g * D + dh * 512:g * D + dh * 512 + 512],
                    op=ALU.add), deps=[last] + yev))
        sev2 = []
        for g in range(GR):
            mo = (g % 2) * 4
            so_ = (g % 2) * 12
            fev_g = fev[g * 2:g * 2 + 2]
            s1 = op("v", lambda e, g=g, so_=so_: e.bn_stats(
                out=T("STAT")[:, so_:so_ + 6],
                in_=sb["RT"].ap()[:, g * D:g * D + 512]),
                deps=fev_g)
            s2 = op("v", lambda e, g=g, so_=so_: e.bn_stats(
                out=T("STAT")[:, so_ + 6:so_ + 12],
                in_=sb["RT"].ap()[:, g * D + 512:g * D + D]), deps=fev_g + [s1])
            s3 = op("v", lambda e, mo=mo, so_=so_: e.bn_aggr(
                out=T("MV")[:, mo:mo + 2], in_=T("STAT")[:, so_:so_ + 12]),
                deps=[s1, s2])
            s4 = op("s", lambda e, mo=mo: e.activation(
                out=T("MV")[:, mo + 2:mo + 3], in_=T("MV")[:, mo + 1:mo + 2],
                func=AF.Ln, bias=EPS), deps=[s3])
            s5 = op("s", lambda e, mo=mo: e.activation(
                out=T("MV")[:, mo + 2:mo + 3], in_=T("MV")[:, mo + 2:mo + 3],
                func=AF.Exp, scale=-0.5), deps=[s4])
            sap = op("v", lambda e, g=g, mo=mo: e.tensor_scalar(
                out=sb["X1F"].ap()[:, g * D:g * D + D],
                in0=sb["RT"].ap()[:, g * D:g * D + D],
                scalar1=T("MV")[:, mo:mo + 1], scalar2=T("MV")[:, mo + 2:mo + 3],
                op0=ALU.subtract, op1=ALU.mult), deps=[s5])
            sev2.append(sap)
        prev_aln = sev2 + hprev[-4:] + pfe
        prev_adc = [[sev2[dc // 4], pfe[-1]] for dc in range(8)]
        prev_pgh = hprev[-2:]
        prev_fq = qfe + [qfmm[-1], fev[-1]]
        tsos = []
        for g in range(GR):
            tsos.append(dma("q", lambda e, b=b, st=st, g=g: e.dma_start(
                out=out_hbm[b, st * tile_s + g * 128:
                            st * tile_s + (g + 1) * 128, :],
                in_=sb["X1F"].ap()[:, g * D:(g + 1) * D]),
                deps=[sev2[g]] + tsos[-1:]))
        prev_tso = [tsos[-1]]
        prev_bar = sch.barrier()

    sems = {}
    ctxs = []
    for s in ["t", "v", "s", "g", "q", "dq", "dg", "dg2"]:
        cm = nc.semaphore(s)
        sems[s] = cm.__enter__()
        ctxs.append(cm)
    with nc.Block() as block:
        @block.tensor
        def _(e):
            sch.emit_engine("t", e, sems)

        @block.vector
        def _(e):
            sch.emit_engine("v", e, sems)

        @block.scalar
        def _(e):
            sch.emit_engine("s", e, sems)

        @block.gpsimd
        def _(e):
            sch.emit_engine("g", e, sems)

        @block.sync
        def _(e):
            sch.emit_engine("q", e, sems)
            e.wait_ge(sems["dq"], sch.dcnt["dq"])
            e.wait_ge(sems["dg"], sch.dcnt["dg"])
            e.wait_ge(sems["dg2"], sch.dcnt["dg2"])
    return DUMP_OUTS


def _host_consts():
    c = {}
    tril_s = np.tril(np.ones((64, 64), np.float32), -1)
    triu_s = np.triu(np.ones((64, 64), np.float32), 1)
    triu_i = np.triu(np.ones((64, 64), np.float32), 0)
    iden = np.eye(64, dtype=np.float32)
    rep = lambda m: np.tile(m, (1, 8))
    bf = lambda m: m.astype(ml_dtypes.bfloat16)
    c["cm_tril_s"] = bf(rep(tril_s))
    c["cm_triu_s"] = bf(rep(triu_s))
    c["cm_triu_i"] = bf(rep(triu_i))
    c["cm_irep"] = bf(np.concatenate([rep(iden), rep(iden)], 0))
    sle_t = np.triu(np.ones((64, 64), np.float32), 0)
    a1 = -sle_t
    a2 = (np.arange(64)[:, None] <= 31).astype(np.float32) * np.ones((64, 64), np.float32) - sle_t
    a3 = -np.tril(np.ones((64, 64), np.float32), -1)
    cum = np.concatenate([a1, a2, a3, np.zeros((64, 64), np.float32)], 1)
    c["c_cums"] = np.ascontiguousarray(np.concatenate([cum, cum], 0))
    aug = np.zeros((128, H * 64), np.float32)
    for h in range(H):
        aug[64:128, h * 64:(h + 1) * 64] = -np.eye(64)
    c["c_auginit"] = bf(aug)
    c["c_ones128"] = bf(np.ones((128, 1), np.float32))
    c["c_ones1r"] = bf(np.ones((1, 128), np.float32))
    c["c_ones1rf"] = np.ones((1, 128), np.float32)
    c["c_id128f"] = np.eye(128, dtype=np.float32)
    esel = np.zeros((48, 3 * 8 * 128), np.float32)
    for ty in range(3):
        for hp in range(8):
            base = (ty * 8 + hp) * 128
            esel[ty * 16 + 2 * hp, base:base + 64] = 1.0
            esel[ty * 16 + 2 * hp + 1, base + 64:base + 128] = 1.0
    c["c_esel"] = bf(esel)
    c["c_fvals"] = np.tile(np.array([[1.0 / D, EPS, 0.5, -0.5, -1.0, 1e-6,
                                      1.0, 0.125, 2.0, 1e-30, 1.0 / 64, 0.0,
                                      0, 0, 0, 0]], np.float32), (128, 1))
    return c


def host_inputs(inputs, nb=2, seq=2048, tile_s=512, core=0):
    x = np.ascontiguousarray(
        inputs["L0_attn_x"][core * nb:(core + 1) * nb, :seq])
    TPB = seq // tile_s
    xfm = np.ascontiguousarray(
        x.reshape(nb, TPB, tile_s // 128, 128, 8, 128)
        .transpose(0, 1, 4, 5, 2, 3)
        .reshape(nb, TPB, 8, 128, tile_s))
    col = lambda v: np.ascontiguousarray(v.reshape(8, 128).T)
    colf = lambda v, n: np.ascontiguousarray(v.reshape(n, 128).T)
    bfc = lambda v: v.astype(ml_dtypes.bfloat16)
    d = dict(
        xfm=xfm,
        wq=inputs["L0_attn_Wq"] * 1.0, wk=inputs["L0_attn_Wk"],
        wv=inputs["L0_attn_Wv"], wo=inputs["L0_attn_Wo"],
        wbg=np.ascontiguousarray(
            np.concatenate([inputs["L0_attn_Wb"], inputs["L0_attn_Wg"]], 1)),
        w1=inputs["L0_attn_w1"], w2=inputs["L0_attn_w2"],
        u1=inputs["L0_ffn_U1"], v1=inputs["L0_ffn_V1"],
        u2=inputs["L0_ffn_U2"], v2=inputs["L0_ffn_V2"],
        g1c=col(inputs["L0_attn_ln1_g"]), g2c=col(inputs["L0_attn_ln2_g"]),
        b1ac=colf(inputs["L0_attn_b1"], 32), b1fc=colf(inputs["L0_ffn_b1"], 32),
        b2ac=col(inputs["L0_attn_b2"]), bocc=col(inputs["L0_attn_bo"]),
        b2fr=inputs["L0_ffn_b2"].reshape(1, D),
        b1lcb=bfc(col(inputs["L0_attn_ln1_b"])),
        b2lcb=bfc(col(inputs["L0_attn_ln2_b"])),
    )
    d.update(_host_consts())
    return d


def kernel(**inputs):
    nc = bass.Bass("TRN2", target_bir_lowering=False, debug=False)
    build(nc)
    in_maps = [host_inputs(inputs, core=c) for c in range(NCORE)]
    res = run_bass_kernel_spmd(nc, in_maps, core_ids=list(range(NCORE)))
    out = np.concatenate([np.asarray(r["out"], dtype=np.float32)
                          for r in res.results], axis=0)
    return out
